# version 06 (final)
# baseline (speedup 1.0000x reference)
import math

import numpy as np
import ml_dtypes

import concourse.bass as bass
import concourse.mybir as mybir
from concourse.bass_utils import run_bass_kernel_spmd

F32 = mybir.dt.float32
BF16 = mybir.dt.bfloat16
I32 = mybir.dt.int32
OP = mybir.AluOpType
AF = mybir.ActivationFunctionType
BF_NP = ml_dtypes.bfloat16

FULL = dict(D=4096, T=4096, B=2, NH=32, NKV=8, ROPE_BASE=500000.0)
MASK_NEG = -30000.0


def cfg_derived(cfg):
    c = dict(cfg)
    c["DC"] = cfg["D"] // 128
    c["TPC"] = cfg["B"] * cfg["T"] // 8
    c["NTT"] = c["TPC"] // 512
    c["NQT"] = cfg["T"] // 512
    c["NHC"] = cfg["NH"] // 4
    c["KVC"] = cfg["NKV"] // 4
    c["GQ"] = cfg["NH"] // cfg["NKV"]
    c["KVROWS"] = cfg["NKV"] * 128
    c["QROWS"] = cfg["NH"] * 128
    c["KVIN_ROWS"] = 2 * c["KVROWS"] + 1
    c["OG_FFN"] = cfg["D"] // 256
    c["OG_K"] = c["KVROWS"] // 256
    c["CTXB"] = c["GQ"] * 128
    return c


class Sem:
    __slots__ = ("name", "n", "h")

    def __init__(self, name):
        self.name, self.n, self.h = name, 0, None


class Prog:
    ENGS = ("t", "v", "s", "g", "y")

    def __init__(self):
        self.phases = [{k: [] for k in self.ENGS}]
        self.sems = []
        self.paranoid = False

    def phase_break(self):
        self.phases.append({k: [] for k in self.ENGS})

    @property
    def ops(self):
        return self.phases[-1]

    def sem(self, name):
        s = Sem(name)
        self.sems.append(s)
        return s

    def op(self, eng, fn, waits=(), inc=None, inc_n=1):
        if self.paranoid:
            m = {}
            for s, v in tuple(waits) + tuple((s, s.n) for s in self.sems):
                if v > 0 and (s not in m or v > m[s]):
                    m[s] = v
            waits = tuple(m.items())
        waits = tuple((s, v) for (s, v) in waits if v > 0)
        if inc is not None:
            inc.n += inc_n

        def g(e, fn=fn, waits=waits, inc=inc, inc_n=inc_n):
            for s, v in waits:
                e.wait_ge(s.h, v)
            r = fn(e)
            if inc is not None:
                r.then_inc(inc.h, inc_n)

        self.ops[eng].append(g)

    def wait(self, eng, waits):
        waits = tuple((s, v) for (s, v) in waits if v > 0)
        if not waits:
            return

        def g(e, waits=waits):
            for s, v in waits:
                e.wait_ge(s.h, v)

        self.ops[eng].append(g)

    def raw(self, eng, fn):
        self.ops[eng].append(fn)


def build_program(cfg, max_phase=99):
    import os
    max_phase = int(os.environ.get("K_MAX_PHASE", max_phase))

    class _Stop(Exception):
        pass

    c = cfg_derived(cfg)
    D, T, DC, TPC, NTT = c["D"], c["T"], c["DC"], c["TPC"], c["NTT"]
    NQT, NHC, KVC, GQ = c["NQT"], c["NHC"], c["KVC"], c["GQ"]
    NH, NKV = c["NH"], c["NKV"]
    KVROWS, QROWS, KVIN_ROWS = c["KVROWS"], c["QROWS"], c["KVIN_ROWS"]
    OG_FFN, OG_K, CTXB = c["OG_FFN"], c["OG_K"], c["CTXB"]
    ISQ = 1.0 / math.sqrt(128.0)
    RG = [[0, 1, 2, 3], [4, 5, 6, 7]]

    nc = bass.Bass("TRN2", target_bir_lowering=False, debug=False,
                   detect_race_conditions=False,
                   dynamic_dma_scratch_size=12288)

    def din(name, shape, dt=F32):
        return nc.dram_tensor(name, list(shape), dt, kind="ExternalInput")

    x_t = din("x_t", (D, TPC))
    wffn_t = din("wffn_t", (D, D))
    wq = din("wq", (D, D))
    wk = din("wk", (D, KVROWS))
    wv = din("wv", (D, KVROWS))
    wo = din("wo", (D, D))
    bias_r = din("bias_r", (128, DC))
    acat_t = din("acat_t", (D, 48))
    bcat_t = din("bcat_t", (48, D))
    blockind = din("blockind", (1, 144))
    lsi_in = din("lsi", (1, 4), I32)
    bval_in = din("bval", (1, 1))
    kidx_in = din("kidx", (128, 1), I32)
    qidx_in = din("qidx", (128, 1), I32)
    vidx_in = din("vidx", (128, 1), I32)
    cidx_in = din("cidx", (128, 1), I32)
    cs_a_in = din("cs_a", (128, TPC), BF16)
    cs_b_in = din("cs_b", (128, TPC), BF16)
    diagm_in = din("diagmask", (128, 128))
    out_t = nc.dram_tensor("out_t", [D, TPC], F32, kind="ExternalOutput")

    x1_sp = nc.dram_tensor("x1_sp", [D, TPC], F32, kind="Internal")
    x2_sp = nc.dram_tensor("x2_sp", [D, TPC], F32, kind="Internal")
    QR1 = QROWS + 1
    qag_in = [nc.dram_tensor(f"qag_in{t}", [QR1, 512], BF16, kind="Internal")
              for t in range(NTT)]
    qag_out = [nc.dram_tensor(f"qag_out{t}", [4 * QR1, 512], BF16,
                              kind="Internal") for t in range(NTT)]
    kv_in = nc.dram_tensor("kv_in", [KVIN_ROWS, TPC], BF16, kind="Internal")
    kv_out = nc.dram_tensor("kv_out", [4 * KVIN_ROWS, TPC], BF16,
                            kind="Internal")
    ctx_in = [nc.dram_tensor(f"ctx_in{t}", [CTXB, T], BF16, kind="Internal")
              for t in range(KVC)]
    ctx_out = [nc.dram_tensor(f"ctx_out{t}", [4 * CTXB, T], BF16,
                              kind="Internal") for t in range(KVC)]
    v_local = nc.dram_tensor("v_local", [T, KVC * 128], BF16, kind="Internal")

    assert DC * 256 >= 2 * T, "wbuf donors must fit two [128,T] k/v slabs"
    assert DC * 512 >= 4096, "h0 donor must fit P3 small buffers"
    xres = nc.alloc_sbuf_tensor("xres", [128, DC * 512], F32)
    h0 = nc.alloc_sbuf_tensor("h0", [128, DC * 512], BF16)
    h1 = nc.alloc_sbuf_tensor("h1", [128, DC * 512], BF16) if NTT > 1 else h0
    HB = [h0, h1]
    wb0 = nc.alloc_sbuf_tensor("wb0", [128, DC * 256], BF16)
    wb1 = nc.alloc_sbuf_tensor("wb1", [128, DC * 256], BF16)
    WB = [wb0, wb1]

    kslab = [wb0[:, i * T:(i + 1) * T] for i in range(2)]
    vslab = [wb1[:, i * T:(i + 1) * T] for i in range(2)]
    qsl = [h0[:, i * 512:(i + 1) * 512] for i in range(2)]
    ebuf = [h0[:, 1024 + i * 1024:1024 + (i + 1) * 1024] for i in range(2)]
    ctxsb = [h0[:, 3072 + i * 512:3072 + (i + 1) * 512] for i in range(2)]
    rdbc = nc.alloc_sbuf_tensor("rdbc", [128, 512], F32)
    brow = nc.alloc_sbuf_tensor("brow", [1, 16], F32)
    ctxg = [HB[i] for i in range(NTT)]

    p2pool = nc.alloc_sbuf_tensor("p2pool", [128, 4096], BF16)
    rp_raw = [p2pool[:, i * 512:(i + 1) * 512] for i in range(2)]
    rp_swp = [p2pool[:, (2 + i) * 512:(3 + i) * 512] for i in range(2)]
    rp_t2 = [p2pool[:, (4 + i) * 512:(5 + i) * 512] for i in range(2)]
    rp_out = [p2pool[:, (6 + i) * 512:(7 + i) * 512] for i in range(2)]
    stat_a = [p2pool[:, i * 512:(i + 1) * 512] for i in range(2)]
    stat_q = [p2pool[:, (2 + i) * 512:(3 + i) * 512] for i in range(2)]
    bcat_s = p2pool

    cs_a = nc.alloc_sbuf_tensor("cs_as", [128, TPC], BF16)
    cs_b = nc.alloc_sbuf_tensor("cs_bs", [128, TPC], BF16)
    tmpf = [nc.alloc_sbuf_tensor(f"tmpf{i}", [128, 512], F32) for i in range(2)]
    mu_bc = nc.alloc_sbuf_tensor("mu_bc", [128, 512], F32)
    is_bc = nc.alloc_sbuf_tensor("is_bc", [128, 512], F32)
    vstage = [nc.alloc_sbuf_tensor(f"vstage{i}", [128, 256], BF16)
              for i in range(2)]
    ostage = [nc.alloc_sbuf_tensor(f"ostage{i}", [128, 512], F32)
              for i in range(2)]
    rows = nc.alloc_sbuf_tensor("rows", [1, 2048], F32)
    R_MU, R_VAR, R_INV, R_T1 = 0, 512, 1024, 1536
    R_LN, R_ST = 0, 512
    rowsb = nc.alloc_sbuf_tensor("rowsb", [1, 512], BF16)
    nstat = nc.alloc_sbuf_tensor("nstat", [1, 64], F32)
    nstat_bf = nc.alloc_sbuf_tensor("nstat_bf", [1, 64], BF16)
    zrow = nc.alloc_sbuf_tensor("zrow", [1, TPC], BF16)
    statrow = nc.alloc_sbuf_tensor("statrow", [1, 256], BF16)
    biasc = nc.alloc_sbuf_tensor("biasc", [128, DC], F32)
    halfb = nc.alloc_sbuf_tensor("halfb", [128, DC], F32)
    bcol = nc.alloc_sbuf_tensor("bcol", [128, 8], F32)
    acat_s = nc.alloc_sbuf_tensor("acat_s", [128, DC * 48], BF16)
    rbf = [nc.alloc_sbuf_tensor(f"rbf{i}", [48, 512], BF16) for i in range(NTT)]
    diagm = nc.alloc_sbuf_tensor("diagm", [128, 128], F32)
    bli = nc.alloc_sbuf_tensor("bli", [1, 144], F32)
    lsi_s = nc.alloc_sbuf_tensor("lsi_s", [1, 4], I32)
    lsi_f = nc.alloc_sbuf_tensor("lsi_f", [1, 4], F32)
    bval_s = nc.alloc_sbuf_tensor("bval_s", [1, 1], F32)
    idx_s = {kk: nc.alloc_sbuf_tensor(f"idx_{kk}", [128, 1], I32)
             for kk in ("k", "q", "v", "c")}
    selw = nc.alloc_sbuf_tensor("selw", [1, 16], F32)
    maskcol = nc.alloc_sbuf_tensor("maskcol", [48, 1], F32)
    ones1f = nc.alloc_sbuf_tensor("ones1f", [1, 128], F32)
    ones1b = nc.alloc_sbuf_tensor("ones1b", [1, 128], BF16)
    onescb = nc.alloc_sbuf_tensor("onescb", [128, 1], BF16)

    pAB = [nc.alloc_psum_tensor(f"pAB{i}", [128, 1024], F32) for i in range(2)]
    pctx = [nc.alloc_psum_tensor(f"pctx{i}", [128, 512], F32) for i in range(2)]
    pd = [nc.alloc_psum_tensor(f"pd{i}", [128, 512], F32) for i in range(2)]

    P = Prog()
    c_pe = P.sem("c_pe")
    c_dv = P.sem("c_dv")
    c_sc = P.sem("c_sc")
    c_gp = P.sem("c_gp")
    c_ly = P.sem("c_ly")
    c_lw = P.sem("c_lw")
    c_cc = P.sem("c_cc")
    ECTR = {"t": c_pe, "v": c_dv, "s": c_sc, "g": c_gp, "y": c_ly}

    def barrier():
        P.phase_break()

    def igather(out_ap, tensor, key, coef, eoff, run, waits=()):
        total = int(np.prod([int(s) for s in tensor.shape]))
        assert total % coef == 0

        def f(e):
            return e.indirect_dma_start(
                out_ap, None,
                bass.AP(tensor, 0, [[coef, total // coef], [1, coef]]),
                bass.IndirectOffsetOnAxis(idx_s[key][:, 0:1], 0),
                element_offset=eoff)
        P.op("g", f, waits=tuple(waits) + ((c_ly, ld_consts),),
             inc=c_lw, inc_n=16)

    def dma_y(out_ap, in_ap, waits=()):
        P.op("y", lambda e, o=out_ap, i=in_ap: e.dma_start(out=o, in_=i),
             waits=waits, inc=c_ly, inc_n=16)

    def dma_g(out_ap, in_ap, waits=()):
        P.op("g", lambda e, o=out_ap, i=in_ap: e.dma_start(out=o, in_=i),
             waits=waits, inc=c_lw, inc_n=16)

    try:
        dma_y(cs_a[:, :], cs_a_in[:, :])
        dma_y(cs_b[:, :], cs_b_in[:, :])
        dma_y(diagm[:, :], diagm_in[:, :])
        dma_y(biasc[:, :], bias_r[:, :])
        dma_y(bli[:, :], blockind[:, :])
        dma_y(lsi_s[:, :], lsi_in[:, :])
        dma_y(bval_s[:, :], bval_in[:, :])
        dma_y(idx_s["k"][:, :], kidx_in[:, :])
        dma_y(idx_s["q"][:, :], qidx_in[:, :])
        dma_y(idx_s["v"][:, :], vidx_in[:, :])
        dma_y(idx_s["c"][:, :], cidx_in[:, :])
        ld_consts = c_ly.n
        dma_g(acat_s[:, :],
              bass.AP(acat_t, 0, [[48, 128], [48 * 128, DC], [1, 48]]))
        ld_acat = c_lw.n

        P.op("g", lambda e: e.memset(zrow[:, :], 0.0), inc=c_gp)
        P.op("g", lambda e: e.memset(ones1f[:, :], 1.0), inc=c_gp)
        P.op("g", lambda e: e.memset(ones1b[:, :], 1.0), inc=c_gp)
        P.op("g", lambda e: e.memset(onescb[:, :], 1.0), inc=c_gp)
        P.op("g", lambda e: e.memset(nstat[:, :], 0.0), inc=c_gp)
        P.wait("g", [(c_ly, ld_consts)])
        gp_setup = c_gp.n

        P.op("v", lambda e: e.tensor_copy(lsi_f[:, :], lsi_s[:, :]),
             waits=[(c_ly, ld_consts)], inc=c_dv)

        def v_op(fn):
            P.op("v", fn, waits=[(c_dv, c_dv.n)], inc=c_dv)

        sw = selw
        v_op(lambda e: e.tensor_single_scalar(sw[0:1, 0:4], lsi_f[0:1, 0:4],
                                              bval_s[0:1, 0:1], OP.is_le))
        v_op(lambda e: e.tensor_scalar(sw[0:1, 4:5], sw[0:1, 1:2], -1.0, 1.0,
                                       OP.mult, OP.add))
        v_op(lambda e: e.tensor_mul(sw[0:1, 5:6], sw[0:1, 4:5], sw[0:1, 0:1]))
        v_op(lambda e: e.tensor_scalar_mul(sw[0:1, 6:7], sw[0:1, 1:2], 2.0))
        v_op(lambda e: e.tensor_add(sw[0:1, 7:8], sw[0:1, 5:6], sw[0:1, 6:7]))
        v_op(lambda e: e.tensor_scalar_add(sw[0:1, 8:9], sw[0:1, 3:4], 3.0))
        v_op(lambda e: e.tensor_sub(sw[0:1, 9:10], sw[0:1, 8:9], sw[0:1, 7:8]))
        v_op(lambda e: e.tensor_mul(sw[0:1, 10:11], sw[0:1, 2:3], sw[0:1, 9:10]))
        v_op(lambda e: e.tensor_add(sw[0:1, 13:14], sw[0:1, 7:8], sw[0:1, 10:11]))
        v_op(lambda e: e.tensor_single_scalar(sw[0:1, 14:15], sw[0:1, 13:14], 2.0,
                                              OP.is_ge))
        v_op(lambda e: e.tensor_scalar(sw[0:1, 15:16], sw[0:1, 13:14], 2.0, 0.0,
                                       OP.subtract, OP.max))
        v_op(lambda e: e.tensor_scalar_min(sw[0:1, 15:16], sw[0:1, 15:16], 2.0))
        for j in range(3):
            v_op(lambda e, j=j: e.tensor_single_scalar(
                sw[0:1, 4 + j:5 + j], sw[0:1, 15:16], float(j), OP.is_equal))
            v_op(lambda e, j=j: e.tensor_mul(
                sw[0:1, 4 + j:5 + j], sw[0:1, 4 + j:5 + j], sw[0:1, 14:15]))
        sel_done = c_dv.n
        for j in range(3):
            P.op("t", lambda e, j=j: e.matmul(
                pd[0][0:48, 0:1], bli[0:1, j * 48:(j + 1) * 48],
                sw[0:1, 4 + j:5 + j], start=(j == 0), stop=(j == 2)),
                waits=[(c_dv, sel_done)] if j == 0 else (), inc=c_pe)
        P.op("v", lambda e: e.tensor_copy(maskcol[:, 0:1], pd[0][0:48, 0:1]),
             waits=[(c_pe, c_pe.n)], inc=c_dv)
        P.op("v", lambda e: e.tensor_scalar_mul(halfb[:, :], biasc[:, :], 0.5),
             inc=c_dv)
        setup_dv = c_dv.n

        def load_res_chunks(src, tt, extra=()):
            ex = tuple(extra) + ((c_dv, c_dv.n), (c_gp, c_gp.n))
            for grp in range(DC // 4):
                j0 = grp * 4
                ap = bass.AP(src, j0 * 128 * TPC + tt * 512,
                             [[TPC, 128], [128 * TPC, 4], [1, 512]])
                dma_y(xres[:, j0 * 512:(j0 + 4) * 512], ap,
                      waits=ex if grp == 0 else ())
            return c_ly.n

        def ln_pass(tt, h_slot, xld):
            _lncut = int(os.environ.get("K_LN_CUT", 99))
            pe0 = c_pe.n
            for j in range(DC):
                b = j % 2
                w = [(c_ly, xld)]
                if j >= 2:
                    w.append((c_pe, pe0 + 2 * (j - 1)))
                P.op("v", lambda e, j=j, b=b: e.tensor_copy(
                    stat_a[b][:, :], xres[:, j * 512:(j + 1) * 512]),
                    waits=w, inc=c_dv)
                sqw = [(c_dv, c_dv.n)]
                if j >= 2:
                    sqw.append((c_pe, pe0 + 2 * (j - 1)))
                P.op("s", lambda e, b=b: e.activation(
                    stat_q[b][:, :], stat_a[b][:, :], AF.Square),
                    waits=sqw, inc=c_sc)
                P.op("t", lambda e, j=j, b=b: e.matmul(
                    pd[0][0:1, :], onescb[:, 0:1], stat_a[b][:, :],
                    start=(j == 0), stop=(j == DC - 1)),
                    waits=[(c_sc, c_sc.n)], inc=c_pe)
                P.op("t", lambda e, j=j, b=b: e.matmul(
                    pd[1][0:1, :], onescb[:, 0:1], stat_q[b][:, :],
                    start=(j == 0), stop=(j == DC - 1)), inc=c_pe)
            stats_pe = c_pe.n
            if _lncut <= 1:
                return c_gp.n
            rinv = 1.0 / D
            P.op("v", lambda e: e.tensor_scalar_mul(
                rows[0:1, R_MU:R_MU + 512], pd[0][0:1, :], rinv),
                waits=[(c_pe, stats_pe)], inc=c_dv)
            P.op("v", lambda e: e.tensor_scalar_mul(
                rows[0:1, R_VAR:R_VAR + 512], pd[1][0:1, :], rinv), inc=c_dv)
            P.op("v", lambda e: e.tensor_mul(
                rows[0:1, R_T1:R_T1 + 512], rows[0:1, R_MU:R_MU + 512],
                rows[0:1, R_MU:R_MU + 512]), inc=c_dv)
            P.op("v", lambda e: e.tensor_sub(
                rows[0:1, R_VAR:R_VAR + 512], rows[0:1, R_VAR:R_VAR + 512],
                rows[0:1, R_T1:R_T1 + 512]), inc=c_dv)
            P.op("v", lambda e: e.tensor_scalar_add(
                rows[0:1, R_VAR:R_VAR + 512], rows[0:1, R_VAR:R_VAR + 512],
                1e-5), inc=c_dv)
            P.op("s", lambda e: e.activation(
                rows[0:1, R_T1:R_T1 + 512], rows[0:1, R_VAR:R_VAR + 512],
                AF.Sqrt), waits=[(c_dv, c_dv.n)], inc=c_sc)
            P.op("v", lambda e: e.reciprocal(
                rows[0:1, R_INV:R_INV + 512], rows[0:1, R_T1:R_T1 + 512]),
                waits=[(c_sc, c_sc.n)], inc=c_dv)
            P.op("v", lambda e: e.tensor_mul(
                rows[0:1, R_T1:R_T1 + 512], rows[0:1, R_INV:R_INV + 512],
                rows[0:1, R_INV:R_INV + 512]), inc=c_dv)
            P.op("v", lambda e: e.tensor_mul(
                rows[0:1, R_T1:R_T1 + 512], rows[0:1, R_T1:R_T1 + 512],
                rows[0:1, R_VAR:R_VAR + 512]), inc=c_dv)
            P.op("v", lambda e: e.tensor_scalar(
                rows[0:1, R_T1:R_T1 + 512], rows[0:1, R_T1:R_T1 + 512],
                -0.5, 1.5, OP.mult, OP.add), inc=c_dv)
            P.op("v", lambda e: e.tensor_mul(
                rows[0:1, R_INV:R_INV + 512], rows[0:1, R_INV:R_INV + 512],
                rows[0:1, R_T1:R_T1 + 512]), inc=c_dv)
            rows_dv = c_dv.n
            if _lncut <= 2:
                return c_gp.n
            P.op("t", lambda e: e.matmul(
                pAB[0][:, 0:512], ones1f[0:1, :], rows[0:1, R_MU:R_MU + 512],
                start=True, stop=True), waits=[(c_dv, rows_dv)], inc=c_pe)
            P.op("t", lambda e: e.matmul(
                pAB[0][:, 512:1024], ones1f[0:1, :], rows[0:1, R_INV:R_INV + 512],
                start=True, stop=True), inc=c_pe)
            P.op("v", lambda e: e.tensor_copy(mu_bc[:, :], pAB[0][:, 0:512]),
                 waits=[(c_pe, c_pe.n)], inc=c_dv)
            P.op("v", lambda e: e.tensor_copy(is_bc[:, :], pAB[0][:, 512:1024]),
                 inc=c_dv)
            bc_dv = c_dv.n
            if _lncut <= 3:
                return c_gp.n
            gp0 = c_gp.n
            for j in range(DC):
                b = j % 2
                w = [(c_dv, bc_dv)]
                if j >= 2:
                    w.append((c_gp, gp0 + (j - 1)))
                P.op("v", lambda e, j=j, b=b: e.tensor_sub(
                    tmpf[b][:, :], xres[:, j * 512:(j + 1) * 512], mu_bc[:, :]),
                    waits=w, inc=c_dv)
                P.op("g", lambda e, j=j, b=b, h=h_slot: e.tensor_mul(
                    h[:, j * 512:(j + 1) * 512], tmpf[b][:, :], is_bc[:, :]),
                    waits=[(c_dv, c_dv.n)], inc=c_gp)
            return c_gp.n

        def lora_r(tt, h_slot, h_rdy):
            for j in range(DC):
                w = []
                if j == 0:
                    w = [(c_gp, h_rdy), (c_lw, ld_acat), (c_dv, c_dv.n)]
                P.op("t", lambda e, j=j, h=h_slot: e.matmul(
                    pctx[0][0:48, :], acat_s[:, j * 48:(j + 1) * 48],
                    h[:, j * 512:(j + 1) * 512],
                    start=(j == 0), stop=(j == DC - 1)), waits=w, inc=c_pe)
            P.op("v", lambda e, tt=tt: e.tensor_scalar_mul(
                rbf[tt][:, :], pctx[0][0:48, :], maskcol[0:48, 0:1]),
                waits=[(c_pe, c_pe.n), (c_dv, setup_dv)], inc=c_dv)
            return c_dv.n

        def wslab_load(wsrc, og, buf, waits=()):
            width = 256
            for grp in range(4):
                j0 = grp * (DC // 4)
                ncnk = DC // 4
                ap = bass.AP(wsrc, (j0 * 128) * wsrc.shape[1] + og * width,
                             [[wsrc.shape[1], 128],
                              [128 * wsrc.shape[1], ncnk], [1, width]])
                P.op("g", lambda e, ap=ap, buf=buf, j0=j0, ncnk=ncnk:
                     e.dma_start(out=WB[buf][:, j0 * 256:(j0 + ncnk) * 256],
                                 in_=ap),
                     waits=waits if grp == 0 else (), inc=c_lw, inc_n=16)
            return c_lw.n

        def ffn_pass(wsrc, src_res, dst, h_rdy, r_rdy, src_rdy, bc_ld):
            banks = [(pAB[0], 0), (pAB[0], 512), (pAB[1], 0), (pAB[1], 512)]
            pe_done = {}
            bank_free = {}
            first = True
            for og in range(OG_FFN):
                w = []
                if og >= 2:
                    w.append((c_pe, pe_done[og - 2]))
                wld = wslab_load(wsrc, og, og % 2, waits=w)
                for tt in range(NTT):
                    for oc in range(2):
                        co = og * 2 + oc
                        k = (og * NTT * 2 + tt * 2 + oc) % 4
                        pb, poff = banks[k]
                        m = k % 2
                        lw = [(c_ly, src_rdy)] if src_rdy else []
                        if k in bank_free:
                            lw.append((c_dv, bank_free[k]))
                        ap = bass.AP(src_res, co * 128 * TPC + tt * 512,
                                     [[TPC, 128], [1, 512]])
                        dma_y(ostage[m][:, :], ap, waits=lw)
                        xld = c_ly.n
                        mmw = [(c_lw, wld)]
                        if k in bank_free:
                            mmw.append((c_dv, bank_free[k]))
                        if first:
                            mmw.append((c_gp, max(h_rdy)))
                            mmw.append((c_dv, max(r_rdy)))
                            mmw.append((c_lw, bc_ld))
                            first = False
                        for j in range(DC):
                            P.op("t", lambda e, j=j, oc=oc, og=og, tt=tt, pb=pb,
                                 poff=poff: e.matmul(
                                     pb[:, poff:poff + 512],
                                     WB[og % 2][:, (j * 2 + oc) * 128:
                                                (j * 2 + oc + 1) * 128],
                                     HB[tt][:, j * 512:(j + 1) * 512],
                                     start=(j == 0), stop=False),
                                 waits=mmw if j == 0 else (), inc=c_pe)
                        P.op("t", lambda e, co=co, tt=tt, pb=pb, poff=poff:
                             e.matmul(pb[:, poff:poff + 512],
                                      bcat_s[0:48, co * 128:(co + 1) * 128],
                                      rbf[tt][:, :], start=False, stop=True),
                             inc=c_pe)
                        pe_here = c_pe.n
                        P.op("v", lambda e, m=m, pb=pb, poff=poff:
                             e.scalar_tensor_tensor(
                                 ostage[m][:, :], pb[:, poff:poff + 512], 0.5,
                                 ostage[m][:, :], OP.mult, OP.add),
                             waits=[(c_pe, pe_here), (c_ly, xld)], inc=c_dv)
                        P.op("v", lambda e, m=m, co=co: e.tensor_scalar_add(
                            ostage[m][:, :], ostage[m][:, :],
                            halfb[:, co:co + 1]), inc=c_dv)
                        bank_free[k] = c_dv.n
                        dma_y(bass.AP(dst, co * 128 * TPC + tt * 512,
                                      [[TPC, 128], [1, 512]]),
                              ostage[m][:, :], waits=[(c_dv, c_dv.n)])
                pe_done[og] = c_pe.n
            return c_ly.n

        rst = {"i": 0, "swp": {}, "t2": {}, "st": {}, "cmb": {}}

        def rope_norm_evac(pb, tt, dst_dram, dst_row, dst_col, pe_val):
            b = rst["i"] % 2
            rst["i"] += 1
            w = [(c_pe, pe_val)]
            if b in rst["swp"]:
                w.append((c_ly, rst["swp"][b]))
            if b in rst["t2"]:
                w.append((c_gp, rst["t2"][b]))
            if b in rst["cmb"]:
                w.append((c_dv, rst["cmb"][b]))
            P.op("v", lambda e, b=b, pb=pb: e.tensor_copy(
                rp_raw[b][:, :], pb[:, 0:512]), waits=w, inc=c_dv)
            raw_dv = c_dv.n
            dma_y(rp_swp[b][0:64, :], rp_raw[b][64:128, :],
                  waits=[(c_dv, raw_dv)])
            dma_y(rp_swp[b][64:128, :], rp_raw[b][0:64, :])
            swp_ld = c_ly.n
            rst["swp"][b] = swp_ld
            tw = [(c_dv, raw_dv)]
            if b in rst["st"]:
                tw.append((c_ly, rst["st"][b]))
            P.op("v", lambda e, b=b, tt=tt: e.tensor_tensor(
                rp_out[b][:, :], rp_raw[b][:, :],
                cs_a[:, tt * 512:(tt + 1) * 512], OP.mult),
                waits=tw, inc=c_dv)
            rst["cmb"][b] = c_dv.n
            P.op("g", lambda e, b=b, tt=tt: e.tensor_tensor(
                rp_t2[b][:, :], rp_swp[b][:, :],
                cs_b[:, tt * 512:(tt + 1) * 512], OP.mult),
                waits=[(c_ly, swp_ld)], inc=c_gp)
            rst["t2"][b] = c_gp.n
            P.op("v", lambda e, b=b: e.tensor_add(
                rp_out[b][:, :], rp_out[b][:, :], rp_t2[b][:, :]),
                waits=[(c_gp, c_gp.n)], inc=c_dv)
            out_dv = c_dv.n
            dma_y(bass.AP(dst_dram, dst_row * dst_dram.shape[1] + dst_col,
                          [[dst_dram.shape[1], 128], [1, 512]]),
                  rp_out[b][:, :], waits=[(c_dv, out_dv)])
            rst["st"][b] = c_ly.n
            return c_ly.n

        h_rdy = [0] * NTT
        r_rdy = [0] * NTT
        for tt in range(NTT):
            xld = load_res_chunks(x_t, tt)
            h_rdy[tt] = ln_pass(tt, HB[tt], xld)
            r_rdy[tt] = lora_r(tt, HB[tt], h_rdy[tt])
        dma_g(bcat_s[0:48, 0:D], bcat_t[:, :], waits=[(c_pe, c_pe.n)])
        ld_bcat = c_lw.n
        x1_st = ffn_pass(wffn_t, x_t, x1_sp, h_rdy, r_rdy, 0, ld_bcat)
        barrier()
        if max_phase <= 1:
            raise _Stop()

        _p2cut = int(os.environ.get("K_P2_CUT", 99))
        for tt in range(NTT):
            xld = load_res_chunks(x1_sp, tt, extra=[(c_ly, x1_st)])
            if _p2cut <= 0:
                continue
            h_rdy[tt] = ln_pass(tt, HB[tt], xld)
            if _p2cut <= 1:
                break
        barrier()
        if max_phase <= 2:
            raise _Stop()

        kv_st = 0
        P.paranoid = True
        for og in range(OG_K):
            wldk = wslab_load(wk, og, og % 2,
                              waits=[(c_pe, c_pe.n)] if og >= 2 else ())
            for tt in range(NTT):
                for oc in range(2):
                    kvh = og * 2 + oc
                    for j in range(DC):
                        w = []
                        if j == 0:
                            w = [(c_lw, wldk), (c_gp, max(h_rdy)),
                                 (c_dv, c_dv.n)]
                        P.op("t", lambda e, j=j, oc=oc, og=og, tt=tt: e.matmul(
                            pctx[0][:, :],
                            WB[og % 2][:, (j * 2 + oc) * 128:(j * 2 + oc + 1) * 128],
                            HB[tt][:, j * 512:(j + 1) * 512],
                            start=(j == 0), stop=(j == DC - 1)),
                            waits=w, inc=c_pe)
                    kv_st = rope_norm_evac(pctx[0], tt, kv_in,
                                           kvh * 128, tt * 512, c_pe.n)
        for og in range(OG_K):
            wldv = wslab_load(wv, og, og % 2, waits=[(c_pe, c_pe.n)])
            for tt in range(NTT):
                for tc in range(4):
                    vb = tc % 2
                    for j in range(DC):
                        w = []
                        if j == 0:
                            w = [(c_lw, wldv), (c_dv, c_dv.n),
                                 (c_ly, c_ly.n)]
                        P.op("t", lambda e, j=j, og=og, tt=tt, tc=tc, vb=vb:
                             e.matmul(
                                 pctx[vb][:, 0:256],
                                 HB[tt][:, j * 512 + tc * 128:
                                        j * 512 + (tc + 1) * 128],
                                 WB[og % 2][:, j * 256:(j + 1) * 256],
                                 start=(j == 0), stop=(j == DC - 1)),
                             waits=w, inc=c_pe)
                    P.op("v", lambda e, vb=vb: e.tensor_copy(
                        vstage[vb][:, :], pctx[vb][:, 0:256]),
                        waits=[(c_pe, c_pe.n), (c_ly, c_ly.n)], inc=c_dv)
                    dma_y(bass.AP(kv_in,
                                  KVROWS * TPC + (tt * 512 + tc * 128) * KVROWS
                                  + og * 256,
                                  [[KVROWS, 128], [1, 256]]),
                          vstage[vb][:, :], waits=[(c_dv, c_dv.n)])
        dma_y(bass.AP(kv_in, (2 * KVROWS) * TPC, [[TPC, 1], [1, TPC]]),
              zrow[0:1, 0:TPC], waits=[(c_gp, gp_setup)])
        kv_st = c_ly.n
        P.op("g", lambda e: e.collective_compute(
            "AllGather", OP.bypass, replica_groups=RG,
            ins=[kv_in.ap().opt()], outs=[kv_out.ap().opt()]),
            waits=[(c_ly, kv_st)], inc=c_cc)
        cc_kv = c_cc.n

        q_st = [0] * NTT
        for tt in range(NTT):
            for og in range(OG_FFN):
                wldq = wslab_load(wq, og, og % 2, waits=[(c_pe, c_pe.n)])
                for oc in range(2):
                    h = og * 2 + oc
                    for j in range(DC):
                        w = []
                        if j == 0:
                            w = [(c_lw, wldq), (c_dv, c_dv.n)]
                        P.op("t", lambda e, j=j, oc=oc, og=og, tt=tt: e.matmul(
                            pctx[0][:, :],
                            WB[og % 2][:, (j * 2 + oc) * 128:(j * 2 + oc + 1) * 128],
                            HB[tt][:, j * 512:(j + 1) * 512],
                            start=(j == 0), stop=(j == DC - 1)),
                            waits=w, inc=c_pe)
                    q_st[tt] = rope_norm_evac(pctx[0], tt, qag_in[tt],
                                              h * 128, 0, c_pe.n)
            dma_y(bass.AP(qag_in[tt], QROWS * 512, [[512, 1], [1, 512]]),
                  zrow[0:1, 0:512], waits=[(c_gp, gp_setup)])
            q_st[tt] = c_ly.n
            P.op("g", lambda e, tt=tt: e.collective_compute(
                "AllGather", OP.bypass, replica_groups=RG,
                ins=[qag_in[tt].ap().opt()], outs=[qag_out[tt].ap().opt()]),
                waits=[(c_ly, q_st[tt]), (c_cc, c_cc.n)], inc=c_cc)
        cc_q = [cc_kv + 1 + t for t in range(NTT)]

        KB = KVC * 128
        vb_dv = {}
        for r in range(4):
            for j in range(TPC // 128):
                bb = (r * (TPC // 128) + j) % 2
                w = [(c_cc, cc_kv)] if (r == 0 and j == 0) else []
                if bb in vb_dv:
                    w.append((c_ly, vb_dv[bb]))
                igather(vstage[bb][:, 0:KB], kv_out, "v", 128,
                        r * KVIN_ROWS * TPC + KVROWS * TPC + (j * 128) * KVROWS,
                        KB, waits=w)
                dma_y(bass.AP(v_local, (r * TPC + j * 128) * KB, [[KB, 128], [1, KB]]),
                      vstage[bb][:, 0:KB], waits=[(c_lw, c_lw.n)])
                vb_dv[bb] = c_ly.n
        stage_kv = c_ly.n
        barrier()
        if max_phase <= 3:
            raise _Stop()

        kf = {}
        qf = {}
        rd_dv_prev = 0
        for kvh in range(KVC):
            for qt in range(NQT):
                si = qt % 2
                seq = kvh * NQT + qt
                nk = (qt + 1) * 512
                w = [(c_cc, cc_kv)]
                prev = kf.get(seq - 2, 0)
                if prev:
                    w.append((c_pe, prev))
                for rt in range(qt + 1):
                    r, t2 = rt // NTT, rt % NTT
                    igather(kslab[si][:, rt * 512:(rt + 1) * 512], kv_out,
                            "k", TPC, (r * KVIN_ROWS + kvh * 128) * TPC + t2 * 512,
                            512, waits=w if rt == 0 else ())
                k_ld = c_lw.n
                dma_y(vslab[si][:, 0:nk],
                      bass.AP(v_local, kvh * 128,
                              [[KB, 128], [128 * KB, nk // 128], [1, 128]]),
                      waits=[(c_ly, stage_kv)] + ([(c_pe, prev)] if prev else []))
                v_ld = c_ly.n
                for qh in range(GQ):
                    hl = kvh * GQ + qh
                    hseq = seq * GQ + qh
                    qb = hseq % 2
                    r, t2 = qt // NTT, qt % NTT
                    wq_ = [(c_cc, cc_q[t2])]
                    prevq = qf.get(hseq - 2, 0)
                    if prevq:
                        wq_.append((c_pe, prevq))
                    igather(qsl[qb][:, :], qag_out[t2], "q", 512,
                            (r * QR1 + hl * 128) * 512, 512, waits=wq_)
                    qld = c_lw.n
                    exp_done = {}
                    npair = (qt + 1) * 2
                    ctx_pb = pctx[qh % 2]
                    d_pb = pd[qh % 2]
                    for kcp in range(npair):
                        spb = pAB[kcp % 2]
                        w2 = [(c_lw, max(k_ld, qld)), (c_ly, v_ld)]
                        if kcp >= 2:
                            w2.append((c_sc, exp_done[kcp - 2]))
                        for half in range(2):
                            kc = kcp * 2 + half
                            P.op("t", lambda e, kc=kc, half=half, spb=spb, qb=qb,
                                 si=si: e.matmul(
                                     spb[:, half * 512:(half + 1) * 512],
                                     kslab[si][:, kc * 128:(kc + 1) * 128],
                                     qsl[qb][:, :], start=True, stop=True),
                                 waits=w2 if half == 0 else (), inc=c_pe)
                        smm_pe = c_pe.n
                        diag_dv = 0
                        for half in range(2):
                            roff = (kcp * 2 + half) * 128 - qt * 512
                            if 0 <= roff < 512:
                                P.op("v", lambda e, spb=spb, half=half, roff=roff:
                                     e.tensor_add(
                                         spb[:, half * 512 + roff:
                                             half * 512 + roff + 128],
                                         spb[:, half * 512 + roff:
                                             half * 512 + roff + 128],
                                         diagm[:, :]),
                                     waits=[(c_pe, smm_pe)], inc=c_dv)
                                diag_dv = c_dv.n
                        ew = [(c_pe, smm_pe)]
                        if diag_dv:
                            ew.append((c_dv, diag_dv))
                        if kcp >= 2:
                            ew.append((c_pe, c_pe.n))
                        P.op("s", lambda e, kcp=kcp, spb=spb: e.activation(
                            ebuf[kcp % 2][:, :], spb[:, :], AF.Exp,
                            scale=ISQ), waits=ew, inc=c_sc)
                        exp_done[kcp] = c_sc.n
                        z_dv = 0
                        for half in range(2):
                            roff = (kcp * 2 + half) * 128 - qt * 512
                            if 0 < roff < 512:
                                P.op("v", lambda e, kcp=kcp, half=half, roff=roff:
                                     e.memset(ebuf[kcp % 2][:, half * 512:
                                                            half * 512 + roff],
                                              0.0),
                                     waits=[(c_sc, exp_done[kcp])], inc=c_dv)
                                z_dv = c_dv.n
                        avw = [(c_sc, exp_done[kcp])]
                        if z_dv:
                            avw.append((c_dv, z_dv))
                        if kcp == 0:
                            avw.append((c_dv, c_dv.n))
                            avw.append((c_sc, c_sc.n))
                        for half in range(2):
                            kc = kcp * 2 + half
                            P.op("t", lambda e, kc=kc, kcp=kcp, half=half, si=si,
                                 ctx_pb=ctx_pb, last=(kc == npair * 2 - 1):
                                 e.matmul(
                                     ctx_pb[:, :],
                                     vslab[si][:, kc * 128:(kc + 1) * 128],
                                     ebuf[kcp % 2][:, half * 512:(half + 1) * 512],
                                     start=(kc == 0), stop=last),
                                 waits=avw if half == 0 else (), inc=c_pe)
                            P.op("t", lambda e, kc=kc, kcp=kcp, half=half,
                                 d_pb=d_pb, last=(kc == npair * 2 - 1): e.matmul(
                                     d_pb[0:1, :], onescb[:, 0:1],
                                     ebuf[kcp % 2][:, half * 512:(half + 1) * 512],
                                     start=(kc == 0), stop=last),
                                 waits=([(c_dv, rd_dv_prev)]
                                        if (kc == 0 and rd_dv_prev) else ()),
                                 inc=c_pe)
                    qf[hseq] = c_pe.n
                    av_pe = c_pe.n
                    P.op("s", lambda e, d_pb=d_pb: e.activation(
                        rows[0:1, R_LN:R_LN + 512], d_pb[0:1, :], AF.Ln),
                        waits=[(c_pe, av_pe)], inc=c_sc)
                    P.op("s", lambda e: e.activation(
                        rowsb[0:1, :], rows[0:1, R_LN:R_LN + 512], AF.Exp,
                        scale=-1.0), inc=c_sc)
                    rd_sc = c_sc.n
                    bcb = pd[(qh + 1) % 2]
                    P.op("t", lambda e, bcb=bcb: e.matmul(
                        bcb[:, :], ones1b[0:1, :], rowsb[0:1, :],
                        start=True, stop=True),
                        waits=[(c_sc, rd_sc), (c_dv, c_dv.n)], inc=c_pe)
                    P.op("v", lambda e, bcb=bcb: e.tensor_copy(
                        rdbc[:, :], bcb[:, :]), waits=[(c_pe, c_pe.n)], inc=c_dv)
                    rd_dv_prev = c_dv.n
                    P.op("v", lambda e, qb=qb, ctx_pb=ctx_pb: e.tensor_tensor(
                        ctxsb[qb][:, :], ctx_pb[:, :], rdbc[:, :], OP.mult),
                        waits=[(c_ly, c_ly.n)], inc=c_dv)
                    dma_y(bass.AP(ctx_in[kvh], (qh * 128) * T + qt * 512,
                                  [[T, 128], [1, 512]]),
                          ctxsb[qb][:, :], waits=[(c_dv, c_dv.n)])
                kf[seq] = c_pe.n
            P.op("g", lambda e, kvh=kvh: e.collective_compute(
                "AllGather", OP.bypass, replica_groups=RG,
                ins=[ctx_in[kvh].ap().opt()], outs=[ctx_out[kvh].ap().opt()]),
                waits=[(c_ly, c_ly.n), (c_cc, c_cc.n)], inc=c_cc)
        cc_ctx = [cc_q[-1] + 1 + i for i in range(KVC)]
        P.paranoid = False
        barrier()
        if max_phase <= 4:
            raise _Stop()

        seen_half = set()
        for tt in range(NTT):
            for j in range(DC):
                r, lh = j // NHC, j % NHC
                half = lh // GQ
                lq = lh % GQ
                w = []
                if half not in seen_half:
                    seen_half.add(half)
                    w = [(c_cc, cc_ctx[half])]
                igather(ctxg[tt][:, j * 512:(j + 1) * 512], ctx_out[half],
                        "c", 512, (r * CTXB + lq * 128) * T + tt * 512, 512,
                        waits=w)
        ctxg_ld = c_lw.n

        banks = [(pAB[0], 0), (pAB[0], 512), (pAB[1], 0), (pAB[1], 512)]
        bank_free = {}
        pe_done = {}
        for og in range(OG_FFN):
            w = [(c_lw, ctxg_ld), (c_pe, c_pe.n)] if og == 0 else []
            if og >= 2:
                w.append((c_pe, pe_done[og - 2]))
            wld = wslab_load(wo, og, og % 2, waits=w)
            for tt in range(NTT):
                for oc in range(2):
                    co = og * 2 + oc
                    k = (og * NTT * 2 + tt * 2 + oc) % 4
                    pb, poff = banks[k]
                    m = k % 2
                    lw = [(c_ly, x1_st)]
                    if k in bank_free:
                        lw.append((c_dv, bank_free[k]))
                    dma_y(ostage[m][:, :],
                          bass.AP(x1_sp, co * 128 * TPC + tt * 512,
                                  [[TPC, 128], [1, 512]]), waits=lw)
                    xld = c_ly.n
                    mmw = [(c_lw, max(wld, ctxg_ld))]
                    if k in bank_free:
                        mmw.append((c_dv, bank_free[k]))
                    for j in range(DC):
                        P.op("t", lambda e, j=j, oc=oc, og=og, tt=tt, pb=pb,
                             poff=poff: e.matmul(
                                 pb[:, poff:poff + 512],
                                 WB[og % 2][:, (j * 2 + oc) * 128:
                                            (j * 2 + oc + 1) * 128],
                                 ctxg[tt][:, j * 512:(j + 1) * 512],
                                 start=(j == 0), stop=(j == DC - 1)),
                             waits=mmw if j == 0 else (), inc=c_pe)
                    P.op("v", lambda e, m=m, pb=pb, poff=poff: e.tensor_add(
                        ostage[m][:, :], ostage[m][:, :], pb[:, poff:poff + 512]),
                        waits=[(c_pe, c_pe.n), (c_ly, xld)], inc=c_dv)
                    bank_free[k] = c_dv.n
                    dma_y(bass.AP(x2_sp, co * 128 * TPC + tt * 512,
                                  [[TPC, 128], [1, 512]]),
                          ostage[m][:, :], waits=[(c_dv, c_dv.n)])
            pe_done[og] = c_pe.n
        x2_st = c_ly.n
        barrier()
        if max_phase <= 5:
            raise _Stop()

        for tt in range(NTT):
            xld = load_res_chunks(x2_sp, tt, extra=[(c_ly, x2_st)])
            h_rdy[tt] = ln_pass(tt, HB[tt], xld)
            r_rdy[tt] = lora_r(tt, HB[tt], h_rdy[tt])
        dma_g(bcat_s[0:48, 0:D], bcat_t[:, :], waits=[(c_pe, c_pe.n)])
        ld_bcat2 = c_lw.n
        ffn_pass(wffn_t, x2_sp, out_t, h_rdy, r_rdy, x2_st, ld_bcat2)

        P.wait("y", [(c_ly, c_ly.n)])
        P.wait("g", [(c_ly, c_ly.n)])

    except _Stop:
        pass

    from contextlib import ExitStack
    with ExitStack() as ctx:
        for s in P.sems:
            s.h = ctx.enter_context(nc.semaphore(s.name))
        for pi, ph in enumerate(P.phases):
            if pi > 0:
                nc.all_engine_barrier()

            def run(key, ph=ph):
                def body(e):
                    for fn in ph[key]:
                        fn(e)
                return body

            with nc.Block() as block:
                block.tensor(run("t"))
                block.vector(run("v"))
                block.scalar(run("s"))
                block.gpsimd(run("g"))
                block.sync(run("y"))
    return nc




def make_core_inputs(cfg, core, x, wffn, bias, lora_A, lora_B, lsi,
                     wq, wk, wv, wo):
    c = cfg_derived(cfg)
    D, TPC = cfg["D"], c["TPC"]
    b, blk = core // 4, core % 4
    tok0 = blk * TPC
    xs = np.ascontiguousarray(x[b, tok0:tok0 + TPC, :].T)
    wffn_t = np.ascontiguousarray(wffn.T)
    acat_t = np.ascontiguousarray(lora_A.reshape(48, D).T)
    bcat_t = np.ascontiguousarray(
        lora_B.transpose(0, 2, 1).reshape(48, D))
    bias_r = np.ascontiguousarray(bias.reshape(c["DC"], 128).T)
    blockind = np.zeros((1, 144), np.float32)
    for j in range(3):
        blockind[0, j * 48 + j * 16:j * 48 + (j + 1) * 16] = 1.0
    pos = np.arange(tok0, tok0 + TPC, dtype=np.float64)
    inv = cfg["ROPE_BASE"] ** (-np.arange(0, 128, 2, dtype=np.float64) / 128.0)
    ang = np.outer(inv, pos)
    cosv = np.cos(ang).astype(np.float32)
    sinv = np.sin(ang).astype(np.float32)
    cs_a = np.concatenate([cosv, cosv], 0).astype(BF_NP)
    cs_b = np.concatenate([-sinv, sinv], 0).astype(BF_NP)
    dm = np.where(np.arange(128)[:, None] > np.arange(128)[None, :],
                  np.float32(MASK_NEG), np.float32(0.0)).astype(np.float32)
    T = cfg["T"]
    hbase = blk * c["NHC"] * 128
    kvbase = blk * c["KVC"] * 128
    p = np.arange(128, dtype=np.int64)
    kidx = (kvbase + p).astype(np.int32).reshape(128, 1)
    qidx = (hbase + p).astype(np.int32).reshape(128, 1)
    vidx = (p * cfg["NKV"] + kvbase // 128).astype(np.int32).reshape(128, 1)
    cidx = (p * (T // 512) + tok0 // 512).astype(np.int32).reshape(128, 1)
    return {
        "kidx": kidx, "qidx": qidx, "vidx": vidx, "cidx": cidx,
        "x_t": xs, "wffn_t": wffn_t, "wq": wq, "wk": wk, "wv": wv, "wo": wo,
        "bias_r": bias_r.astype(np.float32), "acat_t": acat_t,
        "bcat_t": bcat_t, "blockind": blockind,
        "lsi": np.asarray(lsi).reshape(1, 4).astype(np.int32),
        "bval": np.array([[float(b)]], np.float32),
        "cs_a": cs_a, "cs_b": cs_b, "diagmask": dm,
    }


def assemble_out(cfg, results):
    c = cfg_derived(cfg)
    out = np.empty((cfg["B"], cfg["T"], cfg["D"]), np.float32)
    for core in range(8):
        b, blk = core // 4, core % 4
        tok0 = blk * c["TPC"]
        out[b, tok0:tok0 + c["TPC"], :] = results[core]["out_t"].T
    return out


_NC_CACHE = {}


def kernel(L0_attn_x, L0_attn_Wq, L0_attn_Wk, L0_attn_Wv, L0_attn_Wo,
           L0_ffn_weight, L0_ffn_bias, L0_ffn_lora_A, L0_ffn_lora_B,
           L0_ffn_lora_start_indices):
    cfg = FULL
    if "full" not in _NC_CACHE:
        _NC_CACHE["full"] = build_program(cfg)
    nc = _NC_CACHE["full"]
    in_maps = [make_core_inputs(
        cfg, core,
        np.asarray(L0_attn_x, np.float32),
        np.asarray(L0_ffn_weight, np.float32),
        np.asarray(L0_ffn_bias, np.float32),
        np.asarray(L0_ffn_lora_A, np.float32),
        np.asarray(L0_ffn_lora_B, np.float32),
        np.asarray(L0_ffn_lora_start_indices),
        np.asarray(L0_attn_Wq, np.float32),
        np.asarray(L0_attn_Wk, np.float32),
        np.asarray(L0_attn_Wv, np.float32),
        np.asarray(L0_attn_Wo, np.float32)) for core in range(8)]
    res = run_bass_kernel_spmd(nc, in_maps, core_ids=list(range(8)))
    return assemble_out(cfg, [res.results[i] for i in range(8)])
